# version 04
# baseline (speedup 1.0000x reference)
import numpy as np
from contextlib import ExitStack

import concourse.bass as bass
import concourse.mybir as mybir
from concourse.bass_utils import run_bass_kernel_spmd

F32 = mybir.dt.float32
BF16 = mybir.dt.bfloat16
AF = mybir.ActivationFunctionType
ALU = mybir.AluOpType
AX = mybir.AxisListType

N_CORES = 8
SEQ = 4096
D = 2048
HID = 5632
DC = D // 128
HC = HID // 128
NH = 16
TB = 512
NTB = SEQ // TB
QB = 256
NQB = SEQ // QB
T128 = SEQ // 128
INV_D = 1.0 / D
LN_EPS = 1e-5
Q_EPS = 1e-8
SM_SCALE = float(1.0 / np.sqrt(np.float32(128.0)))
NEG_BIG = -1e30

PHASES = 3
DEBUG_XNF = False
SKIP_PREPROC = False


class Sem:
    __slots__ = ("name", "handle", "count")

    def __init__(self, name):
        self.name, self.handle, self.count = name, None, 0


class Op:
    __slots__ = ("engine", "fn", "waits", "inc", "inc_amt")

    def __init__(self, engine, fn, waits, inc, inc_amt):
        self.engine, self.fn, self.waits, self.inc, self.inc_amt = engine, fn, waits, inc, inc_amt


class Sched:

    def __init__(self):
        self.ops = []
        self.sems = []
        self.self_sem = {}

    def sem(self, name):
        s = Sem(name)
        self.sems.append(s)
        return s

    def set_self_sems(self, scalar_sem, vector_sem):
        self.self_sem = {"scalar": scalar_sem, "vector": vector_sem}

    def op(self, engine, fn, waits=(), inc=None, inc_amt=1):
        w = [t for t in waits if t is not None]
        if inc is None and engine in self.self_sem:
            inc = self.self_sem[engine]
        self.ops.append(Op(engine, fn, w, inc, inc_amt))
        if inc is not None:
            inc.count += inc_amt
            return (inc, inc.count)
        return None

    def dma(self, engine, fn, waits=(), inc=None):
        return self.op(engine, fn, waits=waits, inc=inc, inc_amt=16)

    def check_deadlock(self):
        queues = {}
        for idx, o in enumerate(self.ops):
            queues.setdefault(o.engine, []).append((idx, o))
        heads = {e: 0 for e in queues}
        counts = {s.name: 0 for s in self.sems}
        chains = dict(self.self_sem)
        progressed = True
        n_done = 0
        total = len(self.ops)
        while n_done < total:
            progressed = False
            for e, q in queues.items():
                while heads[e] < len(q):
                    idx, o = q[heads[e]]
                    if all(counts[s.name] >= v for (s, v) in o.waits):
                        if o.inc is not None:
                            counts[o.inc.name] += o.inc_amt
                        heads[e] += 1
                        n_done += 1
                        progressed = True
                    else:
                        break
            if not progressed:
                print(f"DEADLOCK: {n_done}/{total} ops done")
                for e, q in queues.items():
                    if heads[e] < len(q):
                        idx, o = q[heads[e]]
                        blocked = [(s.name, v, counts[s.name]) for (s, v) in o.waits
                                   if counts[s.name] < v]
                        print(f"  {e}: global-op#{idx} blocked on {blocked}")
                return False
        return True

    def emit(self, nc, stack):
        for s in self.sems:
            s.handle = stack.enter_context(nc.semaphore(s.name))
        block = stack.enter_context(nc.Block())
        for eng_name in ("sync", "scalar", "vector", "tensor", "gpsimd"):
            ops = [o for o in self.ops if o.engine == eng_name]
            if not ops:
                continue
            chain = self.self_sem.get(eng_name)
            counts = []
            run = 0
            for o in ops:
                if chain is not None and o.inc is chain:
                    run += o.inc_amt
                counts.append(run)

            def body(eng, ops=ops, chain=chain, counts=counts):
                hwm = {}
                prev = None
                for o, cnt in zip(ops, counts):
                    waits = list(o.waits)
                    if chain is not None and prev:
                        waits.append((chain, prev))
                    for (s, v) in waits:
                        if hwm.get(s.name, -1) < v:
                            eng.wait_ge(s.handle, v)
                            hwm[s.name] = v
                    instr = o.fn(eng)
                    if o.inc is not None:
                        instr.then_inc(o.inc.handle, o.inc_amt)
                    if chain is not None:
                        prev = cnt
                return None

            getattr(block, eng_name)(body)


class Ring:
    def __init__(self, slots):
        self.slots = list(slots)
        self.n = len(self.slots)
        self.free_tokens = [[] for _ in self.slots]
        self.i = 0

    def acquire(self):
        k = self.i % self.n
        self.i += 1
        w = self.free_tokens[k]
        self.free_tokens[k] = []
        return k, self.slots[k], w

    def release(self, k, tokens):
        self.free_tokens[k].extend(t for t in tokens if t is not None)

    def reset(self, tokens):
        toks = [t for t in tokens if t is not None]
        self.free_tokens = [list(toks) for _ in self.slots]


def build_nc():
    nc = bass.Bass("TRN2", target_bir_lowering=False)

    x_in = nc.dram_tensor("x", [SEQ, D], F32, kind="ExternalInput").ap()
    qw = nc.dram_tensor("qw", [D, D], F32, kind="ExternalInput").ap()
    kw = nc.dram_tensor("kw", [D, D], F32, kind="ExternalInput").ap()
    vw = nc.dram_tensor("vw", [D, D], F32, kind="ExternalInput").ap()
    ow = nc.dram_tensor("ow", [D, D], F32, kind="ExternalInput").ap()
    gw = nc.dram_tensor("gw", [128, 128], F32, kind="ExternalInput").ap()
    gb = nc.dram_tensor("gb", [128], F32, kind="ExternalInput").ap()
    w1 = nc.dram_tensor("w1", [HID, D], F32, kind="ExternalInput").ap()
    w2 = nc.dram_tensor("w2", [HID, D], F32, kind="ExternalInput").ap()
    w3 = nc.dram_tensor("w3", [D, HID], F32, kind="ExternalInput").ap()
    idn = nc.dram_tensor("idn", [128, 128], BF16, kind="ExternalInput").ap()
    msk = nc.dram_tensor("msk", [4, 128, 512], F32, kind="ExternalInput").ap()
    out = nc.dram_tensor("out", [SEQ, D], F32, kind="ExternalOutput").ap()
    DBG_ATTN = False
    _scr_kind = "ExternalOutput" if DBG_ATTN else "Internal"

    qw1t = nc.dram_tensor("qw1t", [HC, 128, D], BF16, kind="Internal").ap()
    qw2t = nc.dram_tensor("qw2t", [HC, 128, D], BF16, kind="Internal").ap()
    qw3t = nc.dram_tensor("qw3t", [4, 11, 128, D], BF16, kind="Internal").ap()
    r1 = nc.dram_tensor("r1", [SEQ, D], F32, kind=_scr_kind).ap()
    r2 = nc.dram_tensor("r2", [SEQ, D], F32, kind=_scr_kind).ap()
    q_scr = nc.dram_tensor("q_scr", [NH, 128, SEQ], BF16, kind=_scr_kind).ap()
    k_scr = nc.dram_tensor("k_scr", [NH, 128, SEQ], BF16, kind=_scr_kind).ap()
    g_scr = nc.dram_tensor("g_scr", [NH, 128, SEQ], BF16, kind=_scr_kind).ap()
    v_scr = nc.dram_tensor("v_scr", [NH, 128, SEQ], BF16, kind=_scr_kind).ap()
    avg_scr = nc.dram_tensor("avg_scr", [NH, 128, SEQ], BF16, kind=_scr_kind).ap()

    S = Sched()
    qL = S.sem("qL")
    qS = S.sem("qS")
    qC = S.sem("qC")
    ePE = S.sem("ePE")
    eACT = S.sem("eACT")
    eDVE = S.sem("eDVE")
    eGP = S.sem("eGP")
    S.set_self_sems(eACT, eDVE)

    stack = ExitStack()
    with stack:
        SB_BASE = 32768

        USE_MANUAL = True

        def at(name, off_bytes, shape, dtype):
            if USE_MANUAL:
                return nc.alloc_sbuf_tensor_at(name, shape, dtype, offset=SB_BASE + off_bytes)
            return nc.alloc_sbuf_tensor(name, shape, dtype)

        idn_sb = at("idn_sb", 0, [128, 128], BF16)
        gwT_sb = at("gwT_sb", 256, [128, 128], BF16)
        gb_sb = at("gb_sb", 512, [128, 1], F32)
        ones_row = at("ones_row", 576, [1, 128], F32)
        ones_col = at("ones_col", 1088, [128, 1], F32)
        th_sb = at("th_sb", 1152, [128, 8], F32)
        lnst = at("lnst", 1216, [128, 16], F32)
        tiny = at("tiny", 1280, [1, 16], F32)
        cst = at("cst", 1344, [128, 4], F32)
        msk_sb = at("msk_sb", 4096, [128, 2048], F32)
        W0 = 16384

        ps_f = [stack.enter_context(nc.psum_tensor(f"psf{i}", [128, 512], F32)) for i in range(6)]
        ps_b = [stack.enter_context(nc.psum_tensor(f"psb{i}", [128, 1024], BF16)) for i in range(2)]
        fring = Ring(ps_f)
        bring = Ring(ps_b)

        _ph = 8 if SKIP_PREPROC else HID
        gbuf = [at(f"gbuf{i}", W0 + i * 22528, [128, _ph], F32) for i in range(2)]
        qtmp = at("qtmp", W0 + 45056, [128, _ph], F32)
        qbf = at("qbf", W0 + 67584, [128, _ph], BF16)
        qstg = at("qstg", W0 + 78848, [128, _ph], BF16)
        gwb = at("gwb", W0 + 90112, [128, 128], BF16)
        gring = Ring(gbuf)

        S.dma("sync", lambda e: e.dma_start(out=idn_sb[:, :], in_=idn), inc=qL)
        S.dma("sync", lambda e: e.dma_start(
            out=gb_sb[:, :], in_=gb.rearrange("(p one) -> p one", one=1)), inc=qL)
        S.dma("sync", lambda e: e.dma_start(
            out=msk_sb[:, :].rearrange("p (b w) -> p b w", b=4),
            in_=msk.transpose([1, 0, 2])), inc=qL)
        consts_tok = (qL, qL.count)
        S.op("vector", lambda e: e.memset(ones_row[:, :], 1.0), inc=eDVE)
        ones_tok = S.op("vector", lambda e: e.memset(ones_col[:, :], 1.0), inc=eDVE)
        S.op("vector", lambda e: e.memset(cst[:, 0:1], LN_EPS), inc=eDVE)
        cst_tok = S.op("vector", lambda e: e.memset(cst[:, 1:2], 0.5 * Q_EPS), inc=eDVE)

        t0 = S.dma("gpsimd", lambda e: e.dma_start(out=gwb[:, :], in_=gw), inc=qC)
        bk, bt, bw = bring.acquire()
        tp = S.op("tensor", lambda e, bt=bt: e.transpose(bt[:, 0:128], gwb[:, :], idn_sb[:, :]),
                  waits=[t0, consts_tok] + bw, inc=ePE)
        ev = S.op("vector", lambda e, bt=bt: e.tensor_copy(gwT_sb[:, :], bt[:, 0:128]),
                  waits=[tp], inc=eDVE)
        bring.release(bk, [ev])

        def gamma_pass(widx, wap, nblk, ncols):
            c0, c1 = widx * 2, widx * 2 + 1
            acc_tok = None
            for b in range(nblk):
                gk, buf, gwaits = gring.acquire()
                ld = S.dma("sync", lambda e, b=b, buf=buf, wap=wap, ncols=ncols: e.dma_start(
                    out=buf[:, 0:ncols], in_=wap[128 * b:128 * (b + 1), :]), waits=gwaits, inc=qL)
                red = S.op("vector", lambda e, buf=buf, ncols=ncols, widx=widx: e.reduce_sum(
                    lnst[:, 8 + widx:9 + widx], buf[:, 0:ncols], axis=AX.X,
                    apply_absolute_value=True), waits=[ld], inc=eDVE)
                gring.release(gk, [red])
                if b == 0:
                    acc_tok = S.op("vector", lambda e, c1=c1, widx=widx: e.tensor_copy(
                        th_sb[:, c1:c1 + 1], lnst[:, 8 + widx:9 + widx]), waits=[red], inc=eDVE)
                else:
                    acc_tok = S.op("vector", lambda e, c1=c1, widx=widx: e.tensor_add(
                        th_sb[:, c1:c1 + 1], th_sb[:, c1:c1 + 1], lnst[:, 8 + widx:9 + widx]),
                        waits=[red], inc=eDVE)
            fk, fb, fw = fring.acquire()
            mm = S.op("tensor", lambda e, fb=fb, c1=c1: e.matmul(
                fb[0:1, 0:1], th_sb[:, c1:c1 + 1], ones_col[:, :], start=True, stop=True),
                waits=[acc_tok, ones_tok] + fw, inc=ePE)
            bkg, btg, bwg = bring.acquire()
            dmy = S.op("tensor", lambda e, btg=btg: e.transpose(
                btg[:, 0:128], idn_sb[:, :], idn_sb[:, :]), waits=bwg + [consts_tok], inc=ePE)
            bring.release(bkg, [dmy])
            nelem = float(HID * D)
            cp0 = S.op("scalar", lambda e, fb=fb, widx=widx: e.activation(
                tiny[0:1, 8 + widx:9 + widx], fb[0:1, 0:1], AF.Copy,
                scale=0.5 / nelem), waits=[mm, dmy], inc=eACT)
            cp = S.op("vector", lambda e, widx=widx: e.tensor_scalar(
                tiny[0:1, widx:widx + 1], tiny[0:1, 8 + widx:9 + widx],
                cst[0:1, 1:2], None, ALU.add), waits=[cp0, cst_tok], inc=eDVE)
            fring.release(fk, [cp])
            fk2, fb2, fw2 = fring.acquire()
            mm2 = S.op("tensor", lambda e, fb2=fb2, widx=widx: e.matmul(
                fb2[:, 0:1], ones_row[0:1, :], tiny[0:1, widx:widx + 1], start=True, stop=True),
                waits=[cp] + fw2, inc=ePE)
            bkg2, btg2, bwg2 = bring.acquire()
            dmy2 = S.op("tensor", lambda e, btg2=btg2: e.transpose(
                btg2[:, 0:128], idn_sb[:, :], idn_sb[:, :]), waits=bwg2, inc=ePE)
            bring.release(bkg2, [dmy2])
            e1 = S.op("vector", lambda e, fb2=fb2, c0=c0: e.tensor_copy(
                th_sb[:, c0:c0 + 1], fb2[:, 0:1]), waits=[mm2, dmy2], inc=eDVE)
            e2 = S.op("scalar", lambda e, fb2=fb2, c1=c1: e.mul(
                th_sb[:, c1:c1 + 1], fb2[:, 0:1], -1.0), waits=[mm2, dmy2], inc=eACT)
            fring.release(fk2, [e1, e2])
            return [e1, e2]

        if not SKIP_PREPROC:
            th1 = gamma_pass(0, w1, HC, D)
            th2 = gamma_pass(1, w2, HC, D)
            th3 = gamma_pass(2, w3, DC, HID)

        def quant_pass(widx, wap, nblk, ncols, thw, store_fn):
            c0, c1 = widx * 2, widx * 2 + 1
            prev_tp = [None]
            prev_store = []
            store_toks = []
            for b in range(nblk):
                gk, buf, gwaits = gring.acquire()
                ld = S.dma("sync", lambda e, b=b, buf=buf, wap=wap, ncols=ncols: e.dma_start(
                    out=buf[:, 0:ncols], in_=wap[128 * b:128 * (b + 1), :]),
                    waits=gwaits + (thw if b < 2 else []), inc=qL)
                lt = S.op("vector", lambda e, buf=buf, ncols=ncols, c1=c1: e.tensor_scalar(
                    qtmp[:, 0:ncols], buf[:, 0:ncols], th_sb[:, c1:c1 + 1], None, ALU.is_lt),
                    waits=[ld] + thw, inc=eDVE)
                qz = S.op("vector", lambda e, buf=buf, ncols=ncols, c0=c0: e.scalar_tensor_tensor(
                    qbf[:, 0:ncols], buf[:, 0:ncols], th_sb[:, c0:c0 + 1], qtmp[:, 0:ncols],
                    ALU.is_gt, ALU.subtract), waits=[lt, prev_tp[0]], inc=eDVE)
                gring.release(gk, [qz])
                nchunk = ncols // 128
                evs = []
                tps = []
                for cg in range(0, nchunk, 8):
                    nj = min(8, nchunk - cg)
                    bk, btile, bws = bring.acquire()
                    tps = []
                    for j in range(nj):
                        c = cg + j
                        tpo = S.op("tensor", lambda e, btile=btile, c=c, j=j: e.transpose(
                            btile[:, 128 * j:128 * (j + 1)], qbf[:, 128 * c:128 * (c + 1)],
                            idn_sb[:, :]),
                            waits=([qz] + bws + prev_store) if j == 0 else [],
                            inc=ePE if j == nj - 1 else None)
                        tps.append(tpo)
                    evo = S.op("vector", lambda e, btile=btile, cg=cg, nj=nj: e.tensor_copy(
                        qstg[:, 128 * cg:128 * (cg + nj)], btile[:, 0:128 * nj]),
                        waits=[tps[-1]], inc=eDVE)
                    bring.release(bk, [evo])
                    evs.append(evo)
                prev_tp[0] = tps[-1]
                prev_store = []
                for (dst_ap, col0, ncol) in store_fn(b):
                    st = S.dma("gpsimd", lambda e, dst_ap=dst_ap, col0=col0, ncol=ncol: e.dma_start(
                        out=dst_ap, in_=qstg[:, col0:col0 + ncol]), waits=evs, inc=qS)
                    prev_store.append(st)
                    store_toks.append(st)
            return store_toks

        if not SKIP_PREPROC:
            w1_toks = quant_pass(0, w1, HC, D, th1, lambda b: [(qw1t[b, :, :], 0, D)])
            w2_toks = quant_pass(1, w2, HC, D, th2, lambda b: [(qw2t[b, :, :], 0, D)])
            w3_toks = quant_pass(2, w3, DC, HID, th3,
                                 lambda b: [(qw3t[c // 11, c % 11, :, 128 * b:128 * (b + 1)], 128 * c, 128)
                                            for c in range(HC)])
        preproc_done = (qS, qS.count)

        hidden = at("hidden", W0, [128, 44 * TB], BF16)
        xnf = at("xnf", W0 + 45056, [128, 16 * TB], BF16)
        xn_tm = [at(f"xn_tm{i}", W0 + 61440 + i * 4096, [128, 2048], BF16) for i in range(2)]
        wt1 = [at(f"wt1_{i}", W0 + 69632 + i * 4096, [128, 2048], BF16) for i in range(2)]
        wt2 = [at(f"wt2_{i}", W0 + 77824 + i * 4096, [128, 2048], BF16) for i in range(2)]
        wt1.append(at("wt1_2", W0 + 167936, [128, 2048], BF16))
        wt2.append(at("wt2_2", W0 + 172032, [128, 2048], BF16))
        w3c = [at(f"w3c{i}", W0 + 86016 + i * 11264, [128, 11 * 512], BF16) for i in range(4)]
        ln_in = [at(f"ln_in{i}", W0 + 131072 + i * 8192, [128, 2048], F32) for i in range(2)]
        rres = [at(f"rres{i}", W0 + 147456 + i * 2048, [128, 512], F32) for i in range(2)]
        ostg = [at(f"ostg{i}", W0 + 151552 + i * 2048, [128, 512], F32) for i in range(2)]
        sig = [at(f"sig{i}", W0 + 155648 + i * 2048, [128, 512], F32) for i in range(2)]
        sqtr = at("sqtr", W0 + 159744, [128, 2048], F32)

        xnf3 = xnf[:, :].rearrange("p (d t) -> p d t", d=16)
        hid3 = hidden[:, :].rearrange("p (h t) -> p h t", h=44)

        lnin_ring = Ring(ln_in)
        xntm_ring = Ring(xn_tm)
        wt1_ring = Ring(wt1)
        wt2_ring = Ring(wt2)
        rres_ring = Ring(rres)
        ostg_ring = Ring(ostg)
        sig_ring = Ring(sig)
        w3c_ring = Ring(w3c)
        ln_state = {"var": None}

        def ln_tile(src_ap, t0, dst3, dst_t, gate_toks, xnf_free, tm_ring=None):
            tm_ring = tm_ring or xntm_ring
            """LayerNorm tokens [t0,t0+128) -> dst3[:, d, dst_t:dst_t+128] bf16."""
            lk, lbuf, lw = lnin_ring.acquire()
            ld = S.dma("sync", lambda e, lbuf=lbuf, t0=t0, src_ap=src_ap: e.dma_start(
                out=lbuf[:, :], in_=src_ap[t0:t0 + 128, :]), waits=lw + gate_toks, inc=qL)
            red = S.op("vector", lambda e, lbuf=lbuf: e.reduce_sum(
                lnst[:, 0:1], lbuf[:, :], axis=AX.X), waits=[ld], inc=eDVE)
            sq = S.op("scalar", lambda e, lbuf=lbuf: e.activation(
                sqtr[:, :], lbuf[:, :], AF.Square, accum_out=lnst[:, 1:2]),
                waits=[ld, ln_state["var"]], inc=eACT)
            m = S.op("vector", lambda e: e.tensor_scalar(
                lnst[:, 2:3], lnst[:, 0:1], INV_D, None, ALU.mult), waits=[red], inc=eDVE)
            m2 = S.op("vector", lambda e: e.tensor_mul(
                lnst[:, 3:4], lnst[:, 2:3], lnst[:, 2:3]), waits=[m], inc=eDVE)
            var = S.op("vector", lambda e: e.scalar_tensor_tensor(
                lnst[:, 4:5], lnst[:, 1:2], INV_D, lnst[:, 3:4], ALU.mult, ALU.subtract),
                waits=[sq, m2], inc=eDVE)
            ln_state["var"] = var
            lg = S.op("scalar", lambda e: e.activation(
                lnst[:, 5:6], lnst[:, 4:5], AF.Ln, bias=cst[:, 0:1]), waits=[var, cst_tok], inc=eACT)
            rstd = S.op("scalar", lambda e: e.activation(
                lnst[:, 6:7], lnst[:, 5:6], AF.Exp, scale=-0.5), waits=[lg], inc=eACT)
            xk, xbuf, xw = tm_ring.acquire()
            nrm = S.op("vector", lambda e, lbuf=lbuf, xbuf=xbuf: e.tensor_scalar(
                xbuf[:, :], lbuf[:, :], lnst[:, 2:3], lnst[:, 6:7], ALU.subtract, ALU.mult),
                waits=[rstd] + xw, inc=eDVE)
            lnin_ring.release(lk, [nrm])
            evs = []
            for cg in (0, 8):
                bk, btile, bws = bring.acquire()
                tps = []
                for j in range(8):
                    c = cg + j
                    tpo = S.op("tensor", lambda e, btile=btile, xbuf=xbuf, c=c, j=j: e.transpose(
                        btile[:, 128 * j:128 * (j + 1)], xbuf[:, 128 * c:128 * (c + 1)],
                        idn_sb[:, :]),
                        waits=([nrm] + bws + xnf_free) if j == 0 else [],
                        inc=ePE if j == 7 else None)
                    tps.append(tpo)
                evo = S.op("vector", lambda e, btile=btile, cg=cg, dst3=dst3, dst_t=dst_t: e.tensor_copy(
                    dst3[:, cg:cg + 8, dst_t:dst_t + 128],
                    btile[:, :].rearrange("p (j f) -> p j f", j=8)),
                    waits=[tps[-1]], inc=eDVE)
                bring.release(bk, [evo])
                evs.append(evo)
            tm_ring.release(xk, evs)
            return evs

        def ffn_phase(src_ap, dst_ap, first_gate, scr_gate, region_gate=None):
            dst_toks = []
            prev_w3_mm = [region_gate]
            hid_last_use = [region_gate]
            for tb in range(NTB):
                xnf_free = [prev_w3_mm[0]]
                ln_evs = []
                for i in range(TB // 128):
                    t00 = tb * TB + i * 128
                    ln_evs += ln_tile(src_ap, t00, xnf3, i * 128,
                                      first_gate if (tb == 0 and i == 0) else [],
                                      xnf_free if i == 0 else [])
                xn_ready = ln_evs[-1]
                if DEBUG_XNF:
                    dmp = S.dma("gpsimd", lambda e, tb=tb: e.dma_start(
                        out=dst_ap[tb * 128:(tb + 1) * 128, :].rearrange("a b -> a b"),
                        in_=xnf[:, 0:2048]), waits=[xn_ready], inc=qS)
                    dst_toks.append(dmp)
                    prev_w3_mm[0] = xn_ready
                    continue
                hid_toks = []
                last_mm_x = None
                for hc in range(HC):
                    k1, b1, w1w = wt1_ring.acquire()
                    k2, b2, w2w = wt2_ring.acquire()
                    wld1 = S.dma("sync", lambda e, hc=hc, b1=b1: e.dma_start(
                        out=b1[:, :], in_=qw1t[hc, :, :]),
                        waits=w1w + ([scr_gate] if tb == 0 and hc == 0 else []), inc=qL)
                    wld2 = S.dma("sync", lambda e, hc=hc, b2=b2: e.dma_start(
                        out=b2[:, :], in_=qw2t[hc, :, :]), waits=w2w, inc=qL)
                    fk1, fb1, fw1 = fring.acquire()
                    fk2, fb2, fw2 = fring.acquire()
                    mw = [wld1, wld2] + ([xn_ready] if hc == 0 else [])
                    mm1 = mm2 = None
                    for d in range(DC):
                        mm1 = S.op("tensor", lambda e, fb1=fb1, b1=b1, d=d: e.matmul(
                            fb1[:, :], b1[:, 128 * d:128 * (d + 1)], xnf3[:, d, :],
                            start=(d == 0), stop=(d == DC - 1)),
                            waits=(mw + fw1 + fw2) if d == 0 else [],
                            inc=ePE if d == DC - 1 else None)
                    for d in range(DC):
                        mm2 = S.op("tensor", lambda e, fb2=fb2, b2=b2, d=d: e.matmul(
                            fb2[:, :], b2[:, 128 * d:128 * (d + 1)], xnf3[:, d, :],
                            start=(d == 0), stop=(d == DC - 1)),
                            waits=[], inc=ePE if d == DC - 1 else None)
                    last_mm_x = mm2
                    sk, sb_, sw = sig_ring.acquire()
                    sg = S.op("scalar", lambda e, fb1=fb1, sb_=sb_: e.activation(
                        sb_[:, :], fb1[:, :], AF.Sigmoid), waits=[mm1] + sw, inc=eACT)
                    hd = S.op("vector", lambda e, fb2=fb2, sb_=sb_, hc=hc: e.scalar_tensor_tensor(
                        hid3[:, hc, :], fb2[:, :], 1.0, sb_[:, :], ALU.mult, ALU.mult),
                        waits=[sg, mm2, hid_last_use[0]], inc=eDVE)
                    sig_ring.release(sk, [hd])
                    fring.release(fk1, [sg])
                    fring.release(fk2, [hd])
                    hid_toks.append(hd)
                    wt1_ring.release(k1, [hd])
                    wt2_ring.release(k2, [hd])
                hid_ready = hid_toks[-1]
                mm = None
                for er in range(4):
                    sub = []
                    for g in range(4):
                        gk3, b3, w3w = w3c_ring.acquire()
                        wld = S.dma("sync", lambda e, g=g, er=er, b3=b3: e.dma_start(
                            out=b3[:, :].rearrange("p (s e) -> p s e", s=11),
                            in_=qw3t[g, :, :, 512 * er:512 * (er + 1)].transpose([1, 0, 2])),
                            waits=w3w + [last_mm_x, hid_ready] +
                                  ([scr_gate] if tb == 0 and er == 0 and g == 0 else []), inc=qL)
                        wld = S.dma("sync", lambda e, g=g, er=er, b3=b3: e.dma_start(
                            out=b3[:, 0:64],
                            in_=qw3t[g, 0, :, 512 * er:512 * er + 64]),
                            waits=[wld], inc=qL)
                        sub.append((gk3, b3, wld))
                    for i in range(TB // 128):
                        t00 = tb * TB + i * 128
                        rk, rb, rw = rres_ring.acquire()
                        rl = S.dma("sync", lambda e, rb=rb, t00=t00, er=er, src_ap=src_ap: e.dma_start(
                            out=rb[:, :], in_=src_ap[t00:t00 + 128, 512 * er:512 * (er + 1)]),
                            waits=rw, inc=qL)
                        fk, fb, fw = fring.acquire()
                        for g in range(4):
                            gk3, b3, wld = sub[g]
                            b33 = b3[:, :].rearrange("p (s e) -> p s e", s=11)
                            for s_ in range(11):
                                hc = g * 11 + s_
                                mm = S.op("tensor", lambda e, fb=fb, b33=b33, hc=hc, s_=s_, i=i: e.matmul(
                                    fb[:, :], hid3[:, hc, i * 128:(i + 1) * 128], b33[:, s_, :],
                                    start=(hc == 0), stop=(hc == HC - 1)),
                                    waits=(fw + [wld, hid_ready]) if s_ == 0 else [],
                                    inc=ePE if hc == HC - 1 else None)
                        ok, ob, owt = ostg_ring.acquire()
                        DBG_NORES = False
                        if DBG_NORES:
                            evo = S.op("vector", lambda e, fb=fb, ob=ob: e.tensor_scalar(
                                ob[:, :], fb[:, :], 0.5, None, ALU.mult),
                                waits=[mm, rl] + owt, inc=eDVE)
                        else:
                            evo = S.op("vector", lambda e, fb=fb, ob=ob, rb=rb: e.scalar_tensor_tensor(
                                ob[:, :], fb[:, :], 0.5, rb[:, :], ALU.mult, ALU.add),
                                waits=[mm, rl] + owt, inc=eDVE)
                        fring.release(fk, [evo])
                        rres_ring.release(rk, [evo])
                        st = S.dma("gpsimd", lambda e, ob=ob, t00=t00, er=er, dst_ap=dst_ap: e.dma_start(
                            out=dst_ap[t00:t00 + 128, 512 * er:512 * (er + 1)], in_=ob[:, :]),
                            waits=[evo], inc=qS)
                        ostg_ring.release(ok, [st])
                        dst_toks.append(st)
                    for g in range(4):
                        gk3, b3, wld = sub[g]
                        w3c_ring.release(gk3, [mm])
                hid_last_use[0] = mm
                prev_w3_mm[0] = mm
            return dst_toks, prev_w3_mm[0]

        xn_a = at("xn_a", W0 + 0, [128, 16 * SEQ], BF16)
        xna3 = xn_a[:, :].rearrange("p (d t) -> p d t", d=16)
        wjit = [at(f"wjit{i}", W0 + 147456 + i * 4096, [128, 2048], BF16) for i in range(2)]
        wqt = [at(f"wqt{i}", W0 + 155648 + i * 4096, [128, 2048], BF16) for i in range(2)]
        qstage = [at(f"qstage{i}", W0 + 163840 + i * 1024, [128, 512], BF16) for i in range(2)]
        gstage = [at(f"gstage{i}", W0 + 165888 + i * 1024, [128, 512], BF16) for i in range(2)]
        vwt = at("vwt", W0 + 131072, [128, 8192], BF16)
        S_f = [at(f"S_f{i}", W0 + i * 16384, [128, 4096], F32) for i in range(2)]
        Pb = [at(f"Pb{i}", W0 + 32768 + i * 8192, [128, 4096], BF16) for i in range(2)]
        Pb2 = [at(f"Pb2{i}", W0 + 49152 + i * 8192, [128, 4096], BF16) for i in range(2)]
        Qb = [at(f"Qb{i}", W0 + 65536 + i * 8192, [128, 4096], BF16) for i in range(2)]
        Kb = [at(f"Kb{i}", W0 + 81920 + i * 8192, [128, 4096], BF16) for i in range(2)]
        Vb = [at(f"Vb{i}", W0 + 98304 + i * 8192, [128, 4096], BF16) for i in range(2)]
        Gb = [at(f"Gb{i}", W0 + 114688 + i * 8192, [128, 4096], BF16) for i in range(2)]
        PT = [at(f"PT{i}", W0 + 131072 + i * 1024, [128, 512], BF16) for i in range(2)]
        avstg = [at(f"avstg{i}", W0 + 133120 + i * 512, [128, 256], BF16) for i in range(2)]
        a_xn_tm = [at(f"a_xn_tm{i}", W0 + 147456 + i * 4096, [128, 2048], BF16) for i in range(2)]
        avg_in = [at(f"avg_in{i}", W0 + i * 16384, [128, 8192], BF16) for i in range(2)]
        owt = at("owt", W0 + 32768, [128, 16 * 2048], BF16)
        orres = [at(f"orres{i}", W0 + 98304 + i * 2048, [128, 512], F32) for i in range(2)]
        oostg = [at(f"oostg{i}", W0 + 102400 + i * 2048, [128, 512], F32) for i in range(2)]
        orres_ring = Ring(orres)
        oostg_ring = Ring(oostg)
        ps_av = ps_f[5]
        sring = Ring(ps_f[:5])

        def attn_phase(src_ap, dst_ap, a_gate_tok, a_region_tok):
            wjit_ring = Ring(wjit)
            wqt_ring = Ring(wqt)
            qstage_ring = Ring(qstage)
            gstage_ring = Ring(gstage)
            a_tm_ring = Ring(a_xn_tm)
            ln_evs = []
            for i in range(T128):
                t0 = i * 128
                ln_evs += ln_tile(src_ap, t0, xna3, t0,
                                  [a_gate_tok] if i == 0 else [],
                                  [a_region_tok] if i == 0 else [],
                                  tm_ring=a_tm_ring)
            xn_ready = ln_evs[-1]
            ln_done_nrm = ln_evs[-1]

            def jit_tiles(wap, row0, dst_buf, load_gate, dst_free):
                jk, jbuf, jw = wjit_ring.acquire()
                cl = S.dma("gpsimd", lambda e, jbuf=jbuf, wap=wap, row0=row0: e.dma_start(
                    out=jbuf[:, :], in_=wap[row0:row0 + 128, :]), waits=jw + load_gate, inc=qC)
                evs = []
                tps_last = None
                for cg in (0, 8):
                    bk, btile, bws = bring.acquire()
                    tps = []
                    for j in range(8):
                        c = cg + j
                        tpo = S.op("tensor", lambda e, btile=btile, jbuf=jbuf, c=c, j=j: e.transpose(
                            btile[:, 128 * j:128 * (j + 1)], jbuf[:, 128 * c:128 * (c + 1)],
                            idn_sb[:, :]),
                            waits=([cl] + bws) if j == 0 else [],
                            inc=ePE if j == 7 else None)
                        tps.append(tpo)
                    evo = S.op("vector", lambda e, btile=btile, cg=cg, dst_buf=dst_buf: e.tensor_copy(
                        dst_buf[:, 128 * cg:128 * (cg + 8)],
                        btile[:, 0:1024]), waits=[tps[-1]] + dst_free, inc=eDVE)
                    bring.release(bk, [evo])
                    evs.append(evo)
                    tps_last = tps[-1]
                wjit_ring.release(jk, [tps_last])
                return evs

            spill = {}
            for kind, wap, scr, with_gate in (("q", qw, q_scr, True), ("k", kw, k_scr, False)):
                for h in range(NH):
                    tk, tbuf, tw = wqt_ring.acquire()
                    jevs = jit_tiles(wap, 128 * h, tbuf,
                                     [a_gate_tok, xn_ready] if (kind == "q" and h == 0) else [],
                                     tw)
                    last_mm = None
                    sts = []
                    for ts_ in range(8):
                        fk, fb, fw = sring.acquire()
                        mm = None
                        for d in range(DC):
                            mm = S.op("tensor", lambda e, fb=fb, tbuf=tbuf, d=d, ts_=ts_: e.matmul(
                                fb[:, :], tbuf[:, 128 * d:128 * (d + 1)],
                                xna3[:, d, 512 * ts_:512 * (ts_ + 1)],
                                start=(d == 0), stop=(d == DC - 1)),
                                waits=(fw + jevs + ([xn_ready] if ts_ == 0 else [])) if d == 0 else [],
                                inc=ePE if d == DC - 1 else None)
                        last_mm = mm
                        qk, qb_, qw_ = qstage_ring.acquire()
                        ev = S.op("vector", lambda e, fb=fb, qb_=qb_: e.tensor_copy(
                            qb_[:, :], fb[:, :]), waits=[mm] + qw_, inc=eDVE)
                        st = S.dma("gpsimd", lambda e, qb_=qb_, scr=scr, h=h, ts_=ts_: e.dma_start(
                            out=scr[h, :, 512 * ts_:512 * (ts_ + 1)], in_=qb_[:, :]),
                            waits=[ev], inc=qS)
                        sts.append(st)
                        if with_gate:
                            gk_, gb2, gw2 = sring.acquire()
                            gmm = S.op("tensor", lambda e, gb2=gb2, qb_=qb_: e.matmul(
                                gb2[:, :], gwT_sb[:, :], qb_[:, :], start=True, stop=True),
                                waits=[ev] + gw2, inc=ePE)
                            sk2, sb2, sw2 = gstage_ring.acquire()
                            sg = S.op("scalar", lambda e, gb2=gb2, sb2=sb2: e.activation(
                                sb2[:, :], gb2[:, :], AF.Sigmoid, bias=gb_sb[:, :]),
                                waits=[gmm] + sw2, inc=eACT)
                            sring.release(gk_, [sg])
                            gst = S.dma("gpsimd", lambda e, sb2=sb2, h=h, ts_=ts_: e.dma_start(
                                out=g_scr[h, :, 512 * ts_:512 * (ts_ + 1)], in_=sb2[:, :]),
                                waits=[sg], inc=qS)
                            gstage_ring.release(sk2, [gst])
                            sts.append(gst)
                            fring_rel_tok = sg
                        qstage_ring.release(qk, [st] + ([gmm] if with_gate else []))
                        sring.release(fk, [ev])
                    wqt_ring.release(tk, [last_mm])
                    spill[(kind, h)] = sts[-1]
            vstage_ring = qstage_ring
            v_last_mm = None
            for er in range(4):
                vevs = []
                for et in range(4):
                    row0 = er * 512 + et * 128
                    sub = vwt[:, :].rearrange("p (d e) -> p d e", d=16)
                    jk, jbuf, jw = wjit_ring.acquire()
                    cl = S.dma("gpsimd", lambda e, jbuf=jbuf, row0=row0: e.dma_start(
                        out=jbuf[:, :], in_=vw[row0:row0 + 128, :]),
                        waits=jw + ([ln_done_nrm] if (er == 0 and et == 0) else []), inc=qC)
                    tps_last = None
                    for cg in (0, 8):
                        bk, btile, bws = bring.acquire()
                        tps = []
                        for j in range(8):
                            d = cg + j
                            tpo = S.op("tensor", lambda e, btile=btile, jbuf=jbuf, d=d, j=j: e.transpose(
                                btile[:, 128 * j:128 * (j + 1)], jbuf[:, 128 * d:128 * (d + 1)],
                                idn_sb[:, :]),
                                waits=([cl] + bws + ([v_last_mm] if v_last_mm and cg == 0 and et == 0 else [])) if j == 0 else [],
                                inc=ePE if j == 7 else None)
                            tps.append(tpo)
                        evo = S.op("vector", lambda e, btile=btile, cg=cg, et=et: e.tensor_copy(
                            sub[:, cg:cg + 8, et * 128:(et + 1) * 128],
                            btile[:, :].rearrange("p (j f) -> p j f", j=8)),
                            waits=[tps[-1]], inc=eDVE)
                        bring.release(bk, [evo])
                        vevs.append(evo)
                        tps_last = tps[-1]
                    wjit_ring.release(jk, [tps_last])
                for i in range(T128):
                    fk, fb, fw = sring.acquire()
                    mm = None
                    for d in range(DC):
                        mm = S.op("tensor", lambda e, fb=fb, d=d, i=i: e.matmul(
                            fb[:, :], xna3[:, d, i * 128:(i + 1) * 128],
                            vwt[:, :].rearrange("p (d e) -> p d e", d=16)[:, d, :],
                            start=(d == 0), stop=(d == DC - 1)),
                            waits=(fw + vevs) if d == 0 else [],
                            inc=ePE if d == DC - 1 else None)
                    v_last_mm = mm
                    qk, qb_, qw_ = vstage_ring.acquire()
                    ev = S.op("vector", lambda e, fb=fb, qb_=qb_: e.tensor_copy(
                        qb_[:, :], fb[:, :]), waits=[mm] + qw_, inc=eDVE)
                    sring.release(fk, [ev])
                    vsts = []
                    for hh in range(4):
                        h = er * 4 + hh
                        st = S.dma("gpsimd", lambda e, qb_=qb_, h=h, hh=hh, i=i: e.dma_start(
                            out=v_scr[h, :, i * 128:(i + 1) * 128],
                            in_=qb_[:, 128 * hh:128 * (hh + 1)]), waits=[ev], inc=qS)
                        vsts.append(st)
                        spill[("v", h)] = st
                    vstage_ring.release(qk, vsts)
            proj_done_pe = v_last_mm

            QKVG = (Qb, Kb, Vb, Gb)
            buf_last_use = [[None] * 4 for _ in range(2)]
            av_free = [None]
            pt_reader = [None, None]
            avst_prev = [None, None]
            sc_state = {"pb_reader": [None, None], "pb2_reader": None, "s_reader": [None, None]}
            for h in range(NH):
                par = h % 2
                loads = []
                for bi, (bufs, scr_, kindk) in enumerate(zip(
                        QKVG, (q_scr, k_scr, v_scr, g_scr), ("q", "k", "v", "g"))):
                    lw = [buf_last_use[par][bi]] if buf_last_use[par][bi] else []
                    gate = [spill.get(("q", h)), spill.get(("k", h)), spill.get(("v", h))]
                    ldo = S.dma("sync", lambda e, bufs=bufs, par=par, scr_=scr_, h=h: e.dma_start(
                        out=bufs[par][:, :], in_=scr_[h, :, :]),
                        waits=lw + gate + ([proj_done_pe] if h < 2 else []), inc=qL)
                    loads.append(ldo)
                ldq, ldk, ldv, ldg = loads
                for qb in range(NQB):
                    kmax = (qb + 1) * QB
                    ptoks = []
                    for qt in range(2):
                        qtg = qb * 2 + qt
                        klen = (qtg + 1) * 128
                        evs = []
                        diag_kb0 = ((klen - 1) // 512) * 512
                        for kb0 in range(0, klen, 512):
                            kw_ = min(512, klen - kb0)
                            if kb0 == diag_kb0 and kw_ == 128:
                                kw_ = 256
                            fk, fb, fw = sring.acquire()
                            mm = S.op("tensor", lambda e, fb=fb, par=par, qtg=qtg, kb0=kb0, kw_=kw_: e.matmul(
                                fb[:, 0:kw_], Qb[par][:, 128 * qtg:128 * (qtg + 1)],
                                Kb[par][:, kb0:kb0 + kw_], start=True, stop=True),
                                waits=[ldq, ldk] + fw, inc=ePE)
                            sr = sc_state["s_reader"][qt]
                            if kb0 == diag_kb0:
                                b_ = (128 * qtg - kb0) // 128
                                ev0 = S.op("scalar", lambda e, fb=fb, qt=qt, kb0=kb0, kw_=kw_: e.activation(
                                    S_f[qt][:, kb0:kb0 + kw_], fb[:, 0:kw_], AF.Copy,
                                    scale=SM_SCALE), waits=[mm, sr], inc=eACT)
                                evo = S.op("vector", lambda e, qt=qt, kb0=kb0, kw_=kw_, b_=b_: e.tensor_add(
                                    S_f[qt][:, kb0:kb0 + kw_], S_f[qt][:, kb0:kb0 + kw_],
                                    msk_sb[:, 512 * b_:512 * b_ + kw_]),
                                    waits=[ev0, consts_tok], inc=eDVE)
                            else:
                                evo = S.op("scalar", lambda e, fb=fb, qt=qt, kb0=kb0, kw_=kw_: e.activation(
                                    S_f[qt][:, kb0:kb0 + kw_], fb[:, 0:kw_], AF.Copy,
                                    scale=SM_SCALE), waits=[mm, sr], inc=eACT)
                            sring.release(fk, [evo])
                            evs.append(evo)
                        mx = S.op("vector", lambda e, qt=qt, klen=klen: e.reduce_max(
                            lnst[:, 8 + qt:9 + qt], S_f[qt][:, 0:klen], axis=AX.X,
                            negate=True), waits=evs, inc=eDVE)
                        pr = sc_state["pb_reader"][qt]
                        ex = S.op("scalar", lambda e, qt=qt, klen=klen: e.activation(
                            Pb[qt][:, 0:klen], S_f[qt][:, 0:klen], AF.Exp,
                            bias=lnst[:, 8 + qt:9 + qt], accum_out=lnst[:, 10 + qt:11 + qt]),
                            waits=[mx, pr], inc=eACT)
                        sc_state["s_reader"][qt] = ex
                        ptoks.append(ex)
                    rc = S.op("vector", lambda e: e.reciprocal(
                        lnst[:, 12:14], lnst[:, 10:12]), waits=ptoks, inc=eDVE)
                    p2r = sc_state["pb2_reader"]
                    sc2 = []
                    for qt in range(2):
                        qtg = qb * 2 + qt
                        klen = (qtg + 1) * 128
                        sc = S.op("vector", lambda e, qt=qt, klen=klen: e.tensor_scalar(
                            Pb2[qt][:, 0:klen], Pb[qt][:, 0:klen], lnst[:, 12 + qt:13 + qt],
                            None, ALU.mult), waits=[rc, p2r], inc=eDVE)
                        sc_state["pb_reader"][qt] = sc
                        if klen < kmax:
                            sc = S.op("vector", lambda e, qt=qt, klen=klen, kmax=kmax: e.memset(
                                Pb2[qt][:, klen:kmax], 0.0), waits=[sc], inc=eDVE)
                        sc2.append(sc)
                    av_mm = None
                    last_tp = None
                    nch = kmax // 128
                    for cp_ in range(nch // 2):
                        bk, btile, bws = bring.acquire()
                        tps = []
                        for j in range(4):
                            c = 2 * cp_ + j // 2
                            qt = j % 2
                            tpo = S.op("tensor", lambda e, btile=btile, qt=qt, c=c, j=j: e.transpose(
                                btile[:, 128 * j:128 * (j + 1)], Pb2[qt][:, 128 * c:128 * (c + 1)],
                                idn_sb[:, :]),
                                waits=((sc2 + [av_free[0]] if cp_ == 0 else []) + bws) if j == 0 else [],
                                inc=ePE if j == 3 else None)
                            tps.append(tpo)
                        last_tp = tps[-1]
                        pk = cp_ % 2
                        evo = S.op("vector", lambda e, btile=btile, pk=pk: e.tensor_copy(
                            PT[pk][:, :], btile[:, 0:512]),
                            waits=[tps[-1], pt_reader[pk]], inc=eDVE)
                        bring.release(bk, [evo])
                        for jj in range(2):
                            c = 2 * cp_ + jj
                            av_mm = S.op("tensor", lambda e, pk=pk, par=par, c=c, jj=jj: e.matmul(
                                ps_av[:, 0:256], Vb[par][:, 128 * c:128 * (c + 1)],
                                PT[pk][:, 256 * jj:256 * (jj + 1)],
                                start=(c == 0), stop=(c == nch - 1)),
                                waits=[evo, ldv], inc=ePE)
                        pt_reader[pk] = av_mm
                    sc_state["pb2_reader"] = last_tp
                    ak = qb % 2
                    gt = S.op("vector", lambda e, ak=ak, par=par, qb=qb: e.scalar_tensor_tensor(
                        avstg[ak][:, :], ps_av[:, 0:256], 1.0, Gb[par][:, QB * qb:QB * (qb + 1)],
                        ALU.mult, ALU.mult), waits=[av_mm, ldg, avst_prev[ak]], inc=eDVE)
                    av_free[0] = gt
                    avst = S.dma("gpsimd", lambda e, ak=ak, h=h, qb=qb: e.dma_start(
                        out=avg_scr[h, :, QB * qb:QB * (qb + 1)], in_=avstg[ak][:, :]),
                        waits=[gt], inc=qS)
                    avst_prev[ak] = avst
                    spill[("avg", h, qb)] = avst
                    if qb == NQB - 1:
                        buf_last_use[par][0] = mm
                        buf_last_use[par][1] = mm
                        buf_last_use[par][2] = av_mm
                        buf_last_use[par][3] = gt
            flash_done = [sc_state["s_reader"][0], sc_state["s_reader"][1],
                          sc_state["pb_reader"][0], sc_state["pb_reader"][1],
                          sc_state["pb2_reader"], av_free[0], (qS, qS.count)]

            owt3 = owt[:, :].rearrange("p (d e) -> p d e", d=16)
            oevs = []
            for et in range(16):
                jk, jbuf, jw = wjit_ring.acquire()
                cl = S.dma("gpsimd", lambda e, jbuf=jbuf, et=et: e.dma_start(
                    out=jbuf[:, :], in_=ow[128 * et:128 * (et + 1), :]),
                    waits=jw + (flash_done if et < 2 else []), inc=qC)
                tps_last = None
                for cg in (0, 8):
                    bk, btile, bws = bring.acquire()
                    tps = []
                    for j in range(8):
                        d = cg + j
                        tpo = S.op("tensor", lambda e, btile=btile, jbuf=jbuf, d=d, j=j: e.transpose(
                            btile[:, 128 * j:128 * (j + 1)], jbuf[:, 128 * d:128 * (d + 1)],
                            idn_sb[:, :]),
                            waits=([cl] + bws + (flash_done if et == 0 and cg == 0 else [])) if j == 0 else [],
                            inc=ePE if j == 7 else None)
                        tps.append(tpo)
                    evo = S.op("vector", lambda e, btile=btile, cg=cg, et=et: e.tensor_copy(
                        owt3[:, cg:cg + 8, et * 128:(et + 1) * 128],
                        btile[:, :].rearrange("p (j f) -> p j f", j=8)),
                        waits=[tps[-1]] + (flash_done if et == 0 and cg == 0 else []), inc=eDVE)
                    bring.release(bk, [evo])
                    oevs.append(evo)
                    tps_last = tps[-1]
                wjit_ring.release(jk, [tps_last])
            dst_toks = []
            avin_ring = Ring(avg_in)
            o_last_mm = None
            for ts_ in range(8):
                ak, ab, aw = avin_ring.acquire()
                alds = []
                for d in range(DC):
                    ald = S.dma("sync", lambda e, ab=ab, d=d, ts_=ts_: e.dma_start(
                        out=ab[:, 512 * d:512 * (d + 1)], in_=avg_scr[d, :, 512 * ts_:512 * (ts_ + 1)]),
                        waits=aw + [spill[("avg", d, 2 * ts_ + 1)]] +
                              (flash_done if ts_ == 0 else []), inc=qL)
                    alds.append(ald)
                for i4 in (1, 2, 3, 0):
                    t00 = ts_ * 512 + i4 * 128
                    for er in range(4):
                        rk, rb, rw = orres_ring.acquire()
                        rl = S.dma("sync", lambda e, rb=rb, t00=t00, er=er, src_ap=src_ap: e.dma_start(
                            out=rb[:, :], in_=src_ap[t00:t00 + 128, 512 * er:512 * (er + 1)]),
                            waits=rw + (flash_done if (ts_ == 0 and i4 == 0 and er == 0) else []),
                            inc=qL)
                        fk, fb, fw = sring.acquire()
                        mm = None
                        for d in range(DC):
                            mm = S.op("tensor", lambda e, fb=fb, ab=ab, d=d, i4=i4, er=er: e.matmul(
                                fb[:, :], ab[:, 512 * d + 128 * i4:512 * d + 128 * (i4 + 1)],
                                owt3[:, d, 512 * er:512 * (er + 1)],
                                start=(d == 0), stop=(d == DC - 1)),
                                waits=(fw + alds + oevs) if d == 0 else [],
                                inc=ePE if d == DC - 1 else None)
                        o_last_mm = mm
                        ok, ob, owt_ = oostg_ring.acquire()
                        evo = S.op("vector", lambda e, fb=fb, ob=ob, rb=rb: e.scalar_tensor_tensor(
                            ob[:, :], fb[:, :], 1.0, rb[:, :], ALU.mult, ALU.add),
                            waits=[mm, rl] + owt_, inc=eDVE)
                        sring.release(fk, [evo])
                        orres_ring.release(rk, [evo])
                        st = S.dma("gpsimd", lambda e, ob=ob, t00=t00, er=er, dst_ap=dst_ap: e.dma_start(
                            out=dst_ap[t00:t00 + 128, 512 * er:512 * (er + 1)], in_=ob[:, :]),
                            waits=[evo], inc=qS)
                        oostg_ring.release(ok, [st])
                        dst_toks.append(st)
                avin_ring.release(ak, [o_last_mm])
            return dst_toks, o_last_mm, evo

        if PHASES == 1:
            ffn_phase(x_in, out, [preproc_done], preproc_done)
        else:
            a_toks, a_mm = ffn_phase(x_in, r1, [preproc_done], preproc_done)
            a_store = (qS, qS.count)
            pb_dst = out if PHASES == 2 else r2
            b_toks, o_mm, o_evo = attn_phase(r1, pb_dst, a_store, a_mm)
            if PHASES >= 3:
                c_gate = (qS, qS.count)
                fring.reset([o_evo])
                ffn_phase(r2, out, [c_gate], o_mm, region_gate=o_mm)

        S.op("sync", lambda e: e.nop(), waits=[(qS, qS.count)])
        assert S.check_deadlock(), "schedule deadlock"
        S.emit(nc, stack)
    return nc


def kernel(L0_attn_x, L0_attn_q_w, L0_attn_k_w, L0_attn_v_w, L0_attn_o_w,
           L0_attn_gate_w, L0_attn_gate_b, L0_ffn_w1, L0_ffn_w2, L0_ffn_w3):
    import ml_dtypes
    nc = build_nc()
    xs = np.ascontiguousarray(np.asarray(L0_attn_x, dtype=np.float32))
    idn = np.eye(128, dtype=np.float32).astype(ml_dtypes.bfloat16)
    p_ = np.arange(128)[:, None]
    j_ = np.arange(512)[None, :]
    msk = np.stack([np.where(j_ <= p_ + 128 * b, 0.0, NEG_BIG).astype(np.float32)
                    for b in range(4)])
    common = {
        "qw": np.ascontiguousarray(np.asarray(L0_attn_q_w, np.float32)),
        "kw": np.ascontiguousarray(np.asarray(L0_attn_k_w, np.float32)),
        "vw": np.ascontiguousarray(np.asarray(L0_attn_v_w, np.float32)),
        "ow": np.ascontiguousarray(np.asarray(L0_attn_o_w, np.float32)),
        "gw": np.ascontiguousarray(np.asarray(L0_attn_gate_w, np.float32)),
        "gb": np.ascontiguousarray(np.asarray(L0_attn_gate_b, np.float32)),
        "w1": np.ascontiguousarray(np.asarray(L0_ffn_w1, np.float32)),
        "w2": np.ascontiguousarray(np.asarray(L0_ffn_w2, np.float32)),
        "w3": np.ascontiguousarray(np.asarray(L0_ffn_w3, np.float32)),
        "idn": idn,
        "msk": msk,
    }
    in_maps = [dict(common, x=xs[i]) for i in range(N_CORES)]
    res = run_bass_kernel_spmd(nc, in_maps, core_ids=list(range(N_CORES)))
    return np.stack([r["out"] for r in res.results], axis=0)
